# version 06
# baseline (speedup 1.0000x reference)
import numpy as np
import concourse.bass as bass
import concourse.mybir as mybir
from concourse.bass_utils import run_bass_kernel_spmd

F32 = mybir.dt.float32
BF16 = mybir.dt.bfloat16
AF = mybir.ActivationFunctionType
ALU = mybir.AluOpType

D = 2048
KC = 16
T = 2048
H = 16
TC = 256
NT = T // TC
EPS_LN = 1e-5
EPS_AT = 1e-6
INV_D = 1.0 / D

DO_PHASES = ("ffn1", "attn", "ffn2")


def build_nc():
    nc = bass.Bass("TRN2", target_bir_lowering=False, detect_race_conditions=False,
                   dynamic_dma_scratch_size=4096)

    x_in = nc.dram_tensor("x_in", [128, KC * T], F32, kind="ExternalInput").ap()
    x_out = nc.dram_tensor("x_out", [128, KC * T], F32, kind="ExternalOutput").ap()
    w_in = {}
    whlo = {}
    for wn in ("wff", "wq", "wk", "wv", "wo"):
        w_in[wn] = nc.dram_tensor(wn, [16, 128, 2048], F32, kind="ExternalInput").ap()
        whlo[wn] = nc.dram_tensor(wn + "_hl", [16, 128, 4096], BF16).ap()
    bias_d = nc.dram_tensor("bias6", [6, 128, 16], F32, kind="ExternalInput").ap()
    eye_d = nc.dram_tensor("eye", [128, 128], F32, kind="ExternalInput").ap()
    kvin = nc.dram_tensor("kvin", [128, H * 129], F32)
    kvout = nc.dram_tensor("kvout", [128, H * 129], F32)

    import contextlib
    stk = contextlib.ExitStack()

    def sb(name, shape, dt=F32):
        return stk.enter_context(nc.sbuf_tensor(name, shape, dt))

    xs = sb("xs", [128, KC * T])
    slab = sb("slab", [128, 2 * 4096], BF16)
    xhi = sb("xhi", [128, KC * TC], BF16)
    xlo = sb("xlo", [128, KC * TC], BF16)
    big = sb("big", [128, KC * TC])
    kv_sb = sb("kv_sb", [128, H * 129])
    nmr_b = sb("nmr_b", [128, T])
    rs_b = sb("rs_b", [128, T])
    ep = sb("ep", [128, 4 * TC])
    cb = sb("cb", [128, 2048], BF16)
    onesb = sb("onesb", [128, 8], BF16)
    arena = sb("arena", [128, 768])

    eye = arena[:, 0:128]
    ones_col = arena[:, 128:129]
    ones_row = arena[0:1, 144:272]
    negones_row = arena[0:1, 272:400]
    bff = arena[:, 400:416]
    scf = arena[:, 416:432]
    bq = arena[:, 432:448]
    bk = arena[:, 448:464]
    bv = arena[:, 464:480]
    bo = arena[:, 480:496]
    b_h = arena[:, 496:512]
    sc_h = arena[:, 512:528]
    wsum_all = arena[:, 528:544]
    wsq_all = arena[:, 544:560]
    kwsums = arena[:, 560:576]
    vwsums = arena[:, 576:592]
    wsv = arena[0:1, 592:720]
    nbq = arena[:, 720:736]
    nbk = arena[:, 736:752]

    ep_a = ep[:, 0 * TC:1 * TC]
    ep_y = ep[:, 1 * TC:2 * TC]
    ep_e0 = ep[:, 2 * TC:3 * TC]
    ep_e1 = ep[:, 3 * TC:4 * TC]
    r0 = ep[0:1, 0 * TC:1 * TC]
    r1 = ep[0:1, 1 * TC:2 * TC]
    r2 = ep[0:1, 2 * TC:3 * TC]
    r3 = ep[0:1, 3 * TC:4 * TC]

    pb = [stk.enter_context(nc.psum_tensor(f"pb{i}", [128, 512], F32)) for i in range(8)]

    SEM = {}
    for s in ("w0", "w1", "wx", "wc", "st", "mm", "dve", "act", "g", "cc"):
        SEM[s] = stk.enter_context(nc.semaphore(s))

    c = dict(mm=0, dve=0, act=0, w0=0, w1=0, wx=0, wc=0, g=0, cc=0, st=0)
    SY, PE, VE, SC, GP = [], [], [], [], []
    bank_free = {i: None for i in range(8)}
    slab_par = [0]
    slab_last_use = [0, 0]

    def wait(lst, sem, val):
        if val > 0:
            lst.append(lambda e, s=SEM[sem], v=val: e.wait_ge(s, v))

    def mm(out, lhsT, rhs, start, stop, inc=False):
        v = 0
        if inc:
            c["mm"] += 1
            v = c["mm"]
        def f(e, o=out, l=lhsT, r=rhs, s=start, p=stop, iv=v):
            i = e.matmul(o, l, r, start=s, stop=p)
            if iv:
                i.then_inc(SEM["mm"], 1)
        PE.append(f)

    def transpose(out, in_, ident, inc=False):
        v = 0
        if inc:
            c["mm"] += 1
            v = c["mm"]
        def f(e, o=out, i_=in_, idn=ident, iv=v):
            i = e.transpose(o, i_, idn)
            if iv:
                i.then_inc(SEM["mm"], 1)
        PE.append(f)

    def ve_op(kind, *args, inc=False):
        v = 0
        if inc:
            c["dve"] += 1
            v = c["dve"]
        def f(e, k=kind, a=args, iv=v):
            i = getattr(e, k)(*a)
            if iv:
                i.then_inc(SEM["dve"], 1)
        VE.append(f)

    def sc_op(kind, *args, inc=False, **kw):
        v = 0
        if inc:
            c["act"] += 1
            v = c["act"]
        def f(e, k=kind, a=args, kw=kw, iv=v):
            i = getattr(e, k)(*a, **kw)
            if iv:
                i.then_inc(SEM["act"], 1)
        SC.append(f)

    def sy_dma(dst, src, sem):
        c[sem] += 16
        def f(e, d=dst, s=src, sm=SEM[sem]):
            e.dma_start(out=d, in_=s).then_inc(sm, 16)
        SY.append(f)

    def gp_dma(dst, src, lst=None):
        c["wc"] += 16
        def f(e, d=dst, s=src):
            e.dma_start(out=d, in_=s).then_inc(SEM["wc"], 16)
        (GP if lst is None else lst).append(f)

    def slab_load(src, ready_wc=0):
        p = slab_par[0] % 2
        slab_par[0] += 1
        Q = SY if p == 0 else SC
        wait(Q, "mm", slab_last_use[p])
        if ready_wc:
            wait(Q, "wc", ready_wc)
        sem = "w0" if p == 0 else "w1"
        c[sem] += 16
        def f(e, d=slab[:, p * 4096:(p + 1) * 4096], s=src, sm=SEM[sem]):
            e.dma_start(out=d, in_=s).then_inc(sm, 16)
        Q.append(f)
        return p, (sem, c[sem])

    def pe_use_bank(i):
        if bank_free[i] is not None:
            wait(PE, *bank_free[i])

    import os
    BARRIER = int(os.environ.get("DK_BARRIER", "0"))

    def barrier():
        wait(PE, "dve", c["dve"])
        wait(PE, "act", c["act"])
        wait(PE, "wc", c["wc"])
        wait(VE, "mm", c["mm"])
        wait(VE, "act", c["act"])
        wait(VE, "wc", c["wc"])
        wait(SC, "mm", c["mm"])
        wait(SC, "dve", c["dve"])

    g_ctr = [0]

    def proj3(slab_p, bank_override=None):
        if bank_override is None:
            bank = g_ctr[0] % 4
            g_ctr[0] += 1
        else:
            bank = bank_override
        pe_use_bank(bank)
        base = slab_p * 4096
        for k in range(16):
            lh = slab[:, base + k * 128:base + (k + 1) * 128]
            ll = slab[:, base + 2048 + k * 128:base + 2048 + (k + 1) * 128]
            rh = xhi[:, k * TC:(k + 1) * TC]
            rl = xlo[:, k * TC:(k + 1) * TC]
            mm(pb[bank][:, 0:TC], lh, rh, start=(k == 0), stop=False)
            mm(pb[bank][:, 0:TC], ll, rh, start=False, stop=False)
            mm(pb[bank][:, 0:TC], lh, rl, start=False, stop=(k == 15),
               inc=(k == 15))
        return bank

    def wsum_mms(slab_p):
        pe_use_bank(4)
        base = slab_p * 4096
        for k in range(16):
            mm(pb[4][0:1, 0:128], onesb[:, 0:1],
               slab[:, base + k * 128:base + (k + 1) * 128],
               start=(k == 0), stop=False)
            mm(pb[4][0:1, 0:128], onesb[:, 0:1],
               slab[:, base + 2048 + k * 128:base + 2048 + (k + 1) * 128],
               start=False, stop=(k == 15), inc=(k == 15))
        wait(VE, "mm", c["mm"])
        ve_op("tensor_copy", wsv, pb[4][0:1, 0:128], inc=True)

    def wsum_fin(dest_col):
        wait(PE, "dve", c["dve"])
        transpose(pb[4][:, 0:1], wsv, eye[0:1, 0:1], inc=True)
        wait(VE, "mm", c["mm"])
        ve_op("tensor_copy", dest_col, pb[4][:, 0:1], inc=True)
        bank_free[4] = ("dve", c["dve"])

    def convert_blocks(src_fn, guard_mm):
        wait(VE, "mm", guard_mm)
        for k in range(16):
            srcap = src_fn(k)
            hi = xhi[:, k * TC:(k + 1) * TC]
            lo = xlo[:, k * TC:(k + 1) * TC]
            ve_op("tensor_copy", hi, srcap)
            ve_op("tensor_tensor", lo, srcap, hi, ALU.subtract,
                  inc=(k == 15))
        return c["dve"]

    conv_units = []
    for wn in ("wff", "wk", "wv", "wq", "wo"):
        for ot in range(16):
            for piece in range(2):
                conv_units.append((wn, ot, piece))
    conv_done_wc = {}
    conv_state = dict(i=0, par=0, last_conv_dve=[0, 0], cb_free_wc=0)

    def pump(stage_base=None):
        if conv_state["i"] >= len(conv_units):
            return
        wn, ot, piece = conv_units[conv_state["i"]]
        conv_state["i"] += 1
        par = conv_state["par"]
        conv_state["par"] ^= 1
        src = w_in[wn][ot][:, piece * 1024:piece * 1024 + 1024]
        if stage_base is None:
            stage = kv_sb[:, par * 1024:(par + 1) * 1024]
        else:
            stage = stage_base[:, par * 1024:(par + 1) * 1024]
        wait(SC, "dve", conv_state["last_conv_dve"][par])
        gp_dma(stage, src, SC)
        wait(VE, "wc", conv_state["cb_free_wc"])
        wait(VE, "wc", c["wc"])
        ve_op("tensor_copy", cb[:, 0:1024], stage)
        ve_op("tensor_tensor", cb[:, 1024:2048], stage, cb[:, 0:1024],
              ALU.subtract, inc=True)
        conv_state["last_conv_dve"][par] = c["dve"]
        wait(SC, "dve", c["dve"])
        gp_dma(whlo[wn][ot][:, piece * 1024:piece * 1024 + 1024], cb[:, 0:1024], SC)
        gp_dma(whlo[wn][ot][:, 2048 + piece * 1024:2048 + piece * 1024 + 1024],
               cb[:, 1024:2048], SC)
        conv_state["cb_free_wc"] = c["wc"]
        if piece == 1:
            conv_done_wc[(wn, ot)] = c["wc"]

    def slab_ready(wn, ot):
        return conv_done_wc.get((wn, ot), 0)

    sy_dma(xs[:, :], x_in[:, :], "wx")
    sy_dma(eye, eye_d[:, :], "wx")
    for j, d_ in enumerate((bff, scf, bq, bk, bv, bo)):
        sy_dma(d_, bias_d[j], "wx")
    ve_op("memset", ones_col, 1.0)
    ve_op("memset", ones_row, 1.0)
    ve_op("memset", negones_row, -1.0)
    ve_op("memset", onesb[:, :], 1.0)
    wait(VE, "wx", c["wx"])
    ve_op("tensor_scalar", b_h, bff, 0.5, None, ALU.mult)
    ve_op("tensor_scalar", sc_h, scf, 0.5, None, ALU.mult)
    ve_op("tensor_scalar", nbq, bq, -1.0, None, ALU.mult)
    ve_op("tensor_scalar", nbk, bk, -1.0, None, ALU.mult, inc=True)

    def stats_phase(do_pump=False):
        wait(PE, "dve", c["dve"])
        wait(SC, "dve", c["dve"])
        wait(VE, "act", c["act"])
        sq_parity_last_mm = [0, 0]
        prev_math_dve = [0]
        for tc in range(NT):
            tsl = slice(tc * TC, tc * TC + TC)
            pe_use_bank(5)
            pe_use_bank(6)
            for k in range(16):
                xsl = xs[:, k * T + tc * TC: k * T + tc * TC + TC]
                sq_dst = ep[:, (k % 2) * TC:(k % 2) * TC + TC]
                wait(SC, "mm", sq_parity_last_mm[k % 2])
                if k == 0:
                    wait(SC, "dve", prev_math_dve[0])
                if tc == 0 and k < 2:
                    wait(SC, "wx", c["wx"])
                sc_op("activation", sq_dst, xsl, AF.Square, inc=True)
                if tc == 0 and k == 0:
                    wait(PE, "wx", c["wx"])
                mm(pb[5][0:1, 0:TC], ones_col, xsl, start=(k == 0), stop=(k == 15))
                wait(PE, "act", c["act"])
                mm(pb[6][0:1, 0:TC], ones_col, sq_dst, start=(k == 0), stop=(k == 15),
                   inc=True)
                sq_parity_last_mm[k % 2] = c["mm"]
                if do_pump and k % 2 == 0:
                    pump()
            wait(VE, "mm", c["mm"])
            ve_op("tensor_scalar", r0, pb[5][0:1, 0:TC], INV_D, None, ALU.mult)
            ve_op("tensor_scalar", r1, pb[6][0:1, 0:TC], INV_D, None, ALU.mult,
                  inc=True)
            pe_sum_free = ("dve", c["dve"])
            ve_op("tensor_tensor", r2, r0, r0, ALU.mult)
            ve_op("tensor_tensor", r3, r1, r2, ALU.subtract)
            ve_op("tensor_scalar", r2, r3, EPS_LN, None, ALU.add, inc=True)
            wait(SC, "dve", c["dve"])
            sc_op("activation", r3, r2, AF.Sqrt, inc=True)
            wait(VE, "act", c["act"])
            ve_op("reciprocal", r2, r3)
            ve_op("tensor_tensor", r3, r0, r2, ALU.mult, inc=True)
            prev_math_dve[0] = c["dve"]
            wait(PE, "dve", c["dve"])
            wait(PE, *pe_sum_free)
            mm(pb[5][:, 0:TC], negones_row, r3, start=True, stop=True, inc=True)
            mm(pb[6][:, 0:TC], ones_row, r2, start=True, stop=True, inc=True)
            wait(VE, "mm", c["mm"])
            ve_op("tensor_copy", nmr_b[:, tsl], pb[5][:, 0:TC])
            ve_op("tensor_copy", rs_b[:, tsl], pb[6][:, 0:TC], inc=True)
            bank_free[5] = ("dve", c["dve"])
            bank_free[6] = ("dve", c["dve"])

    def ffn_phase(compute_wsum, do_pump=False):
        wait(VE, "act", c["act"])
        last_flush = [0]
        act3_last = [0, 0]
        PA = (ep_a, ep_e0)
        PY = (ep_y, ep_e1)
        xconv_last_mm = [0]
        for tc in range(NT):
            csl = slice(tc * TC, tc * TC + TC)
            xconv_dve = convert_blocks(
                lambda k, tc=tc: xs[:, k * T + tc * TC: k * T + tc * TC + TC],
                xconv_last_mm[0])
            flush_wait_act = 0
            for ot in range(16):
                p, w = slab_load(whlo["wff"][ot], slab_ready("wff", ot))
                wait(PE, *w)
                if ot == 0:
                    wait(PE, "dve", xconv_dve)
                do_ws = compute_wsum and tc == 0
                if do_ws:
                    wsum_mms(p)
                bank = proj3(p)
                slab_last_use[p] = c["mm"]
                if do_ws:
                    wsum_fin(wsum_all[:, ot:ot + 1])
                a_r, y_r = PA[ot % 2], PY[ot % 2]
                wait(VE, "mm", c["mm"])
                if act3_last[ot % 2]:
                    wait(VE, "act", act3_last[ot % 2])
                ve_op("tensor_tensor", a_r, pb[bank][:, 0:TC], rs_b[:, csl], ALU.mult,
                      inc=True)
                bank_free[bank] = ("dve", c["dve"])
                ve_op("scalar_tensor_tensor", y_r, nmr_b[:, csl], wsum_all[:, ot:ot + 1],
                      a_r, ALU.mult, ALU.add, inc=True)
                wait(SC, "dve", c["dve"])
                if last_flush[0]:
                    wait(SC, "dve", last_flush[0])
                sc_op("activation", big[:, ot * TC:(ot + 1) * TC], y_r, AF.Identity,
                      bias=b_h[:, ot:ot + 1], scale=sc_h[:, ot:ot + 1], inc=True)
                act3_last[ot % 2] = c["act"]
                flush_wait_act = c["act"]
                if do_pump and ot % 2 == 0:
                    pump()
            xconv_last_mm[0] = c["mm"]
            wait(VE, "act", flush_wait_act)
            for ot in range(16):
                xsl = xs[:, ot * T + tc * TC: ot * T + tc * TC + TC]
                ve_op("tensor_tensor", xsl, xsl, big[:, ot * TC:(ot + 1) * TC],
                      ALU.add, inc=(ot == 15))
            last_flush[0] = c["dve"]

    def fold_chunk(bank, tsl, wsum_ap, dest, bias_ap, elu, nbias_ap=None):
        wait(VE, "mm", c["mm"])
        ve_op("tensor_tensor", ep_a, pb[bank][:, 0:TC], rs_b[:, tsl], ALU.mult,
              inc=True)
        bank_free[bank] = ("dve", c["dve"])
        ve_op("scalar_tensor_tensor", ep_y, nmr_b[:, tsl], wsum_ap, ep_a,
              ALU.mult, ALU.add, inc=True)
        if elu:
            wait(SC, "dve", c["dve"])
            sc_op("activation", ep_e0, ep_y, AF.Relu, bias=nbias_ap, scale=-1.0)
            sc_op("activation", ep_e1, ep_e0, AF.Exp, scale=-1.0)
            sc_op("activation", ep_e0, ep_y, AF.Relu, bias=bias_ap, scale=1.0,
                  inc=True)
            wait(VE, "act", c["act"])
            ve_op("tensor_tensor", dest, ep_e1, ep_e0, ALU.add, inc=True)
        else:
            ve_op("tensor_scalar", dest, ep_y, bias_ap, None, ALU.add, inc=True)

    def attn_phase():
        wait(VE, "act", c["act"])
        kfm_k = big[:, 0:TC]
        kfm_v = big[:, TC:2 * TC]
        kp_off = 2 * TC
        vt_off = 2 * TC + 256
        for tau in range(2):
            co = vt_off + tau * 129 + 128
            ve_op("memset", big[:, co:co + 1], 1.0, inc=(tau == 1))
        ones_ready = c["dve"]

        xconv_last_mm = [0]
        for tc in range(NT):
            tsl = slice(tc * TC, tc * TC + TC)
            xconv_dve = convert_blocks(
                lambda k, tc=tc: xs[:, k * T + tc * TC: k * T + tc * TC + TC],
                xconv_last_mm[0])
            for h in range(16):
                do_ws = (tc == 0)
                if tc == 0:
                    pump(big[:, 1536:3584])
                    pump(big[:, 1536:3584])
                fold_done = {}
                for proj, wn, bias_ap, wsums, dest in (
                    ("k", "wk", bk[:, h:h + 1], kwsums, kfm_k),
                    ("v", "wv", bv[:, h:h + 1], vwsums, kfm_v),
                ):
                    p, w = slab_load(whlo[wn][h], slab_ready(wn, h))
                    wait(PE, *w)
                    if tc == 0 and h == 0 and proj == "k":
                        wait(PE, "dve", ones_ready)
                    if h == 0 and proj == "k":
                        wait(PE, "dve", xconv_dve)
                    if do_ws:
                        wsum_mms(p)
                    bank = proj3(p)
                    slab_last_use[p] = c["mm"]
                    if do_ws:
                        wsum_fin(wsums[:, h:h + 1])
                    fold_chunk(bank, tsl, wsums[:, h:h + 1], dest, bias_ap,
                               elu=(proj == "k"), nbias_ap=nbk[:, h:h + 1])
                    fold_done[proj] = c["dve"]
                for proj, src, tmo, bw in (("k", kfm_k, kp_off, 128),
                                           ("v", kfm_v, vt_off, 129)):
                    wait(PE, "dve", fold_done[proj])
                    for j in range(2):
                        bnk = 5 + j
                        if bank_free[bnk] is not None:
                            wait(PE, *bank_free[bnk])
                        transpose(pb[bnk][:, 0:128], src[:, j * 128:(j + 1) * 128],
                                  eye, inc=(j == 1))
                    wait(SC, "mm", c["mm"])
                    for j in range(2):
                        co = tmo + j * bw
                        sc_op("activation", big[:, co:co + 128],
                              pb[5 + j][:, 0:128], AF.Identity, inc=(j == 1))
                    bank_free[5] = ("act", c["act"])
                    bank_free[6] = ("act", c["act"])
                if bank_free[7] is not None:
                    wait(PE, *bank_free[7])
                wait(PE, "act", c["act"])
                mm(pb[7][:, 0:129], big[:, kp_off:kp_off + 128],
                   big[:, vt_off:vt_off + 129], start=True, stop=False)
                mm(pb[7][:, 0:129], big[:, kp_off + 128:kp_off + 256],
                   big[:, vt_off + 129:vt_off + 258], start=False, stop=True,
                   inc=True)
                wait(VE, "mm", c["mm"])
                if tc == 0:
                    ve_op("tensor_copy", kv_sb[:, h * 129:(h + 1) * 129],
                          pb[7][:, 0:129], inc=True)
                else:
                    ve_op("tensor_tensor", kv_sb[:, h * 129:(h + 1) * 129],
                          kv_sb[:, h * 129:(h + 1) * 129], pb[7][:, 0:129],
                          ALU.add, inc=True)
                bank_free[7] = ("dve", c["dve"])
            xconv_last_mm[0] = c["mm"]

        wait(GP, "dve", c["dve"])
        c["g"] += 16
        GP.append(lambda e: e.dma_start(out=kvin[:, :], in_=kv_sb[:, :]).then_inc(SEM["g"], 16))
        wait(GP, "g", c["g"])
        c["cc"] += 1
        def cc_fn(e):
            e.collective_compute(
                "AllReduce", ALU.add,
                replica_groups=[[0, 1], [2, 3], [4, 5], [6, 7]],
                ins=[kvin.ap()], outs=[kvout.ap()],
            ).then_inc(SEM["cc"], 1)
        GP.append(cc_fn)
        wait(GP, "cc", c["cc"])
        c["g"] += 16
        GP.append(lambda e: e.dma_start(out=kv_sb[:, :], in_=kvout[:, :]).then_inc(SEM["g"], 16))
        kv_ready = c["g"]

        zr0 = (ep[32:33, 0:TC], ep[64:65, 0:TC])
        zr1 = (ep[0:1, 3 * TC:4 * TC], ep[0:1, 0:TC])
        zrb = (ep[:, TC:2 * TC], ep[:, 2 * TC:3 * TC])
        for tc in range(NT):
            tsl = slice(tc * TC, tc * TC + TC)
            xconv_dve = convert_blocks(
                lambda k, tc=tc: xs[:, k * T + tc * TC: k * T + tc * TC + TC],
                xconv_last_mm[0])
            for dt in range(16):
                p, w = slab_load(whlo["wq"][dt], slab_ready("wq", dt))
                wait(PE, *w)
                if dt == 0:
                    wait(PE, "dve", xconv_dve)
                do_ws = (tc == 0)
                if do_ws:
                    wsum_mms(p)
                bank = proj3(p)
                slab_last_use[p] = c["mm"]
                if do_ws:
                    wsum_fin(wsq_all[:, dt:dt + 1])
                fold_chunk(bank, tsl, wsq_all[:, dt:dt + 1],
                           big[:, dt * TC:(dt + 1) * TC], bq[:, dt:dt + 1], elu=True,
                           nbias_ap=nbq[:, dt:dt + 1])
            qp_done = c["dve"]
            if tc == 0:
                wait(PE, "g", kv_ready)
            wait(PE, "dve", qp_done)
            nb = (4, 0)
            zb = (5, 7)
            bb = (6, 1)
            zrec_done = [0] * 16
            zrecb_mm = [0] * 16

            def pe_numer_z(h):
                par = h % 2
                pe_use_bank(nb[par])
                mm(pb[nb[par]][:, 0:TC], kv_sb[:, h * 129:h * 129 + 128],
                   big[:, h * TC:(h + 1) * TC], start=True, stop=True)
                pe_use_bank(zb[par])
                mm(pb[zb[par]][0:1, 0:TC], kv_sb[:, h * 129 + 128:h * 129 + 129],
                   big[:, h * TC:(h + 1) * TC], start=True, stop=True, inc=True)

            def ve_zrec(h):
                par = h % 2
                wait(VE, "mm", c["mm"])
                ve_op("tensor_scalar", zr0[par], pb[zb[par]][0:1, 0:TC], EPS_AT,
                      None, ALU.add)
                ve_op("reciprocal", zr1[par], zr0[par], inc=True)
                zrec_done[h] = c["dve"]
                bank_free[zb[par]] = ("dve", c["dve"])

            def pe_zrecb(h):
                par = h % 2
                wait(PE, "dve", zrec_done[h])
                pe_use_bank(bb[par])
                mm(pb[bb[par]][:, 0:TC], ones_row, zr1[par], start=True, stop=True,
                   inc=True)
                zrecb_mm[h] = c["mm"]

            def ve_attn(h):
                par = h % 2
                wait(VE, "mm", zrecb_mm[h])
                ve_op("tensor_copy", zrb[par], pb[bb[par]][:, 0:TC])
                bank_free[bb[par]] = ("dve", c["dve"] + 1)
                ve_op("tensor_tensor", big[:, h * TC:(h + 1) * TC],
                      pb[nb[par]][:, 0:TC], zrb[par], ALU.mult, inc=True)
                bank_free[nb[par]] = ("dve", c["dve"])

            def ve_aconv(h):
                src = big[:, h * TC:(h + 1) * TC]
                ve_op("tensor_copy", xhi[:, h * TC:(h + 1) * TC], src)
                ve_op("tensor_tensor", xlo[:, h * TC:(h + 1) * TC], src,
                      xhi[:, h * TC:(h + 1) * TC], ALU.subtract, inc=True)

            pe_numer_z(0)
            ve_zrec(0)
            for h in range(1, 16):
                pe_numer_z(h)
                pe_zrecb(h - 1)
                ve_zrec(h)
                ve_attn(h - 1)
                ve_aconv(h - 1)
            pe_zrecb(15)
            ve_attn(15)
            ve_aconv(15)
            aconv_dve = c["dve"]
            for ct in range(16):
                p, w = slab_load(whlo["wo"][ct], slab_ready("wo", ct))
                wait(PE, *w)
                if ct == 0:
                    wait(PE, "dve", aconv_dve)
                bank = proj3(p)
                slab_last_use[p] = c["mm"]
                xsl = xs[:, ct * T + tc * TC: ct * T + tc * TC + TC]
                wait(VE, "mm", c["mm"])
                ve_op("scalar_tensor_tensor", xsl, pb[bank][:, 0:TC], bo[:, ct:ct + 1],
                      xsl, ALU.add, ALU.add, inc=True)
                bank_free[bank] = ("dve", c["dve"])
            xconv_last_mm[0] = c["mm"]

    if "statsdump" in DO_PHASES:
        stats_phase()
        wait(SY, "dve", c["dve"])
        c["st"] += 32
        SY.append(lambda e: e.dma_start(out=x_out[:, 0:2048], in_=nmr_b[:, :]).then_inc(SEM["st"], 16))
        SY.append(lambda e: e.dma_start(out=x_out[:, 2048:4096], in_=rs_b[:, :]).then_inc(SEM["st"], 16))
    if "ffn1" in DO_PHASES:
        stats_phase(do_pump=True)
        ffn_phase(compute_wsum=True, do_pump=True)
    if "attn" in DO_PHASES:
        stats_phase()
        attn_phase()
    if "ffn2" in DO_PHASES:
        stats_phase()
        ffn_phase(compute_wsum=("ffn1" not in DO_PHASES))

    while conv_state["i"] < len(conv_units):
        pump()

    if "statsdump" not in DO_PHASES:
        wait(SY, "dve", c["dve"])
        c["st"] += 16
        SY.append(lambda e: e.dma_start(out=x_out[:, :], in_=xs[:, :]).then_inc(SEM["st"], 16))
    wait(SY, "st", c["st"])

    with stk, nc.Block() as block:
        @block.sync
        def _(sync):
            for f in SY:
                f(sync)

        @block.tensor
        def _(tensor):
            for f in PE:
                f(tensor)

        @block.vector
        def _(vector):
            for f in VE:
                f(vector)

        @block.scalar
        def _(scalar):
            for f in SC:
                f(scalar)

        @block.gpsimd
        def _(gpsimd):
            for f in GP:
                f(gpsimd)

    return nc


def _tile_w(w):
    wT = np.ascontiguousarray(np.asarray(w, dtype=np.float32).T)
    return np.ascontiguousarray(
        wT.reshape(16, 128, 16, 128).transpose(2, 1, 0, 3).reshape(16, 128, 2048))


def _tile_b(b):
    return np.ascontiguousarray(np.asarray(b, dtype=np.float32).reshape(16, 128).T)


def kernel(L0_attn_hidden_states, L0_attn_q_w, L0_attn_q_b, L0_attn_k_w,
           L0_attn_k_b, L0_attn_v_w, L0_attn_v_b, L0_attn_o_w, L0_attn_o_b,
           L0_ffn_weight, L0_ffn_scale_weight, L0_ffn_bias):
    x = np.asarray(L0_attn_hidden_states, dtype=np.float32)
    bias6 = np.stack([
        _tile_b(L0_ffn_bias), _tile_b(L0_ffn_scale_weight), _tile_b(L0_attn_q_b),
        _tile_b(L0_attn_k_b), _tile_b(L0_attn_v_b), _tile_b(L0_attn_o_b)])
    common = {
        "wff": _tile_w(L0_ffn_weight),
        "wq": _tile_w(L0_attn_q_w),
        "wk": _tile_w(L0_attn_k_w),
        "wv": _tile_w(L0_attn_v_w),
        "wo": _tile_w(L0_attn_o_w),
        "bias6": np.ascontiguousarray(bias6),
        "eye": np.eye(128, dtype=np.float32),
    }
    in_maps = []
    for i in range(8):
        b, half = i // 2, i % 2
        xc = x[b, half * T:(half + 1) * T, :]
        xf = np.ascontiguousarray(
            xc.T.reshape(16, 128, T).transpose(1, 0, 2).reshape(128, KC * T))
        in_maps.append(dict(common, x_in=xf))
    nc = build_nc()
    res = run_bass_kernel_spmd(nc, in_maps, core_ids=list(range(8)))
    out = np.empty((4, 4096, D), dtype=np.float32)
    for i in range(8):
        b, half = i // 2, i % 2
        o = res.results[i]["x_out"]
        xT = o.reshape(128, 16, T).transpose(1, 0, 2).reshape(D, T)
        out[b, half * T:(half + 1) * T, :] = xT.T
    return out


# version 07
# speedup vs baseline: 1.4085x; 1.4085x over previous
import numpy as np
import concourse.bass as bass
import concourse.mybir as mybir
from concourse.bass_utils import run_bass_kernel_spmd

F32 = mybir.dt.float32
BF16 = mybir.dt.bfloat16
AF = mybir.ActivationFunctionType
ALU = mybir.AluOpType

D = 2048
KC = 16
T = 2048
H = 16
TC = 256
NT = T // TC
EPS_LN = 1e-5
EPS_AT = 1e-6
INV_D = 1.0 / D

DO_PHASES = ("ffn1", "attn", "ffn2")


def build_nc():
    nc = bass.Bass("TRN2", target_bir_lowering=False, detect_race_conditions=False,
                   dynamic_dma_scratch_size=4096)

    x_in = nc.dram_tensor("x_in", [128, KC * T], F32, kind="ExternalInput").ap()
    x_out = nc.dram_tensor("x_out", [128, KC * T], F32, kind="ExternalOutput").ap()
    w_in = {}
    whlo = {}
    for wn in ("wff", "wq", "wk", "wv", "wo"):
        w_in[wn] = nc.dram_tensor(wn, [16, 128, 2048], F32, kind="ExternalInput").ap()
        whlo[wn] = nc.dram_tensor(wn + "_hl", [16, 128, 4096], BF16).ap()
    bias_d = nc.dram_tensor("bias6", [6, 128, 16], F32, kind="ExternalInput").ap()
    eye_d = nc.dram_tensor("eye", [128, 128], F32, kind="ExternalInput").ap()
    kvin = nc.dram_tensor("kvin", [128, H * 129], F32)
    kvout = nc.dram_tensor("kvout", [128, H * 129], F32)

    import contextlib
    stk = contextlib.ExitStack()

    def sb(name, shape, dt=F32):
        return stk.enter_context(nc.sbuf_tensor(name, shape, dt))

    xs = sb("xs", [128, KC * T])
    slab = sb("slab", [128, 2 * 4096], BF16)
    xhi = sb("xhi", [128, KC * TC], BF16)
    xlo = sb("xlo", [128, KC * TC], BF16)
    big = sb("big", [128, KC * TC])
    kv_sb = sb("kv_sb", [128, H * 129])
    nmr_b = sb("nmr_b", [128, T])
    rs_b = sb("rs_b", [128, T])
    ep = sb("ep", [128, 4 * TC])
    cb = sb("cb", [128, 2048], BF16)
    onesb = sb("onesb", [128, 8], BF16)
    arena = sb("arena", [128, 768])

    eye = arena[:, 0:128]
    ones_col = arena[:, 128:129]
    ones_row = arena[0:1, 144:272]
    negones_row = arena[0:1, 272:400]
    bff = arena[:, 400:416]
    scf = arena[:, 416:432]
    bq = arena[:, 432:448]
    bk = arena[:, 448:464]
    bv = arena[:, 464:480]
    bo = arena[:, 480:496]
    b_h = arena[:, 496:512]
    sc_h = arena[:, 512:528]
    wsum_all = arena[:, 528:544]
    wsq_all = arena[:, 544:560]
    kwsums = arena[:, 560:576]
    vwsums = arena[:, 576:592]
    wsv = arena[0:1, 592:720]
    nbq = arena[:, 720:736]
    nbk = arena[:, 736:752]

    ep_a = ep[:, 0 * TC:1 * TC]
    ep_y = ep[:, 1 * TC:2 * TC]
    ep_e0 = ep[:, 2 * TC:3 * TC]
    ep_e1 = ep[:, 3 * TC:4 * TC]
    r0 = ep[0:1, 0 * TC:1 * TC]
    r1 = ep[0:1, 1 * TC:2 * TC]
    r2 = ep[0:1, 2 * TC:3 * TC]
    r3 = ep[0:1, 3 * TC:4 * TC]

    pb = [stk.enter_context(nc.psum_tensor(f"pb{i}", [128, 512], F32)) for i in range(8)]

    SEM = {}
    for s in ("w0", "w1", "wx", "wc", "st", "mm", "dve", "act", "g", "cc"):
        SEM[s] = stk.enter_context(nc.semaphore(s))

    c = dict(mm=0, dve=0, act=0, w0=0, w1=0, wx=0, wc=0, g=0, cc=0, st=0)
    SY, PE, VE, SC, GP = [], [], [], [], []
    bank_free = {i: None for i in range(8)}
    slab_par = [0]
    slab_last_use = [0, 0]

    def wait(lst, sem, val):
        if val > 0:
            lst.append(lambda e, s=SEM[sem], v=val: e.wait_ge(s, v))

    def mm(out, lhsT, rhs, start, stop, inc=False):
        v = 0
        if inc:
            c["mm"] += 1
            v = c["mm"]
        def f(e, o=out, l=lhsT, r=rhs, s=start, p=stop, iv=v):
            i = e.matmul(o, l, r, start=s, stop=p)
            if iv:
                i.then_inc(SEM["mm"], 1)
        PE.append(f)

    def transpose(out, in_, ident, inc=False):
        v = 0
        if inc:
            c["mm"] += 1
            v = c["mm"]
        def f(e, o=out, i_=in_, idn=ident, iv=v):
            i = e.transpose(o, i_, idn)
            if iv:
                i.then_inc(SEM["mm"], 1)
        PE.append(f)

    def ve_op(kind, *args, inc=False):
        v = 0
        if inc:
            c["dve"] += 1
            v = c["dve"]
        def f(e, k=kind, a=args, iv=v):
            i = getattr(e, k)(*a)
            if iv:
                i.then_inc(SEM["dve"], 1)
        VE.append(f)

    def sc_op(kind, *args, inc=False, **kw):
        v = 0
        if inc:
            c["act"] += 1
            v = c["act"]
        def f(e, k=kind, a=args, kw=kw, iv=v):
            i = getattr(e, k)(*a, **kw)
            if iv:
                i.then_inc(SEM["act"], 1)
        SC.append(f)

    def sy_dma(dst, src, sem):
        c[sem] += 16
        def f(e, d=dst, s=src, sm=SEM[sem]):
            e.dma_start(out=d, in_=s).then_inc(sm, 16)
        SY.append(f)

    def gp_dma(dst, src, lst=None):
        c["wc"] += 16
        def f(e, d=dst, s=src):
            e.dma_start(out=d, in_=s).then_inc(SEM["wc"], 16)
        (GP if lst is None else lst).append(f)

    def slab_load(src, ready_wc=0):
        p = slab_par[0] % 2
        slab_par[0] += 1
        Q = SY
        wait(Q, "mm", slab_last_use[p])
        if ready_wc:
            wait(Q, "wc", ready_wc)
        sem = "w0" if p == 0 else "w1"
        c[sem] += 16
        def f(e, d=slab[:, p * 4096:(p + 1) * 4096], s=src, sm=SEM[sem]):
            e.dma_start(out=d, in_=s).then_inc(sm, 16)
        Q.append(f)
        return p, (sem, c[sem])

    def pe_use_bank(i):
        if bank_free[i] is not None:
            wait(PE, *bank_free[i])

    import os
    BARRIER = int(os.environ.get("DK_BARRIER", "0"))

    def barrier():
        wait(PE, "dve", c["dve"])
        wait(PE, "act", c["act"])
        wait(PE, "wc", c["wc"])
        wait(VE, "mm", c["mm"])
        wait(VE, "act", c["act"])
        wait(VE, "wc", c["wc"])
        wait(SC, "mm", c["mm"])
        wait(SC, "dve", c["dve"])

    g_ctr = [0]

    def proj3(slab_p, bank_override=None):
        if bank_override is None:
            bank = g_ctr[0] % 4
            g_ctr[0] += 1
        else:
            bank = bank_override
        pe_use_bank(bank)
        base = slab_p * 4096
        for k in range(16):
            lh = slab[:, base + k * 128:base + (k + 1) * 128]
            ll = slab[:, base + 2048 + k * 128:base + 2048 + (k + 1) * 128]
            rh = xhi[:, k * TC:(k + 1) * TC]
            rl = xlo[:, k * TC:(k + 1) * TC]
            mm(pb[bank][:, 0:TC], lh, rh, start=(k == 0), stop=False)
            mm(pb[bank][:, 0:TC], ll, rh, start=False, stop=False)
            mm(pb[bank][:, 0:TC], lh, rl, start=False, stop=(k == 15),
               inc=(k == 15))
        return bank

    def wsum_mms(slab_p):
        pe_use_bank(4)
        base = slab_p * 4096
        for k in range(16):
            mm(pb[4][0:1, 0:128], onesb[:, 0:1],
               slab[:, base + k * 128:base + (k + 1) * 128],
               start=(k == 0), stop=False)
            mm(pb[4][0:1, 0:128], onesb[:, 0:1],
               slab[:, base + 2048 + k * 128:base + 2048 + (k + 1) * 128],
               start=False, stop=(k == 15), inc=(k == 15))
        wait(VE, "mm", c["mm"])
        ve_op("tensor_copy", wsv, pb[4][0:1, 0:128], inc=True)

    def wsum_fin(dest_col):
        wait(PE, "dve", c["dve"])
        transpose(pb[4][:, 0:1], wsv, eye[0:1, 0:1], inc=True)
        wait(VE, "mm", c["mm"])
        ve_op("tensor_copy", dest_col, pb[4][:, 0:1], inc=True)
        bank_free[4] = ("dve", c["dve"])

    def convert_blocks(src_fn, guard_mm):
        wait(VE, "mm", guard_mm)
        for k in range(16):
            srcap = src_fn(k)
            hi = xhi[:, k * TC:(k + 1) * TC]
            lo = xlo[:, k * TC:(k + 1) * TC]
            ve_op("tensor_copy", hi, srcap)
            ve_op("tensor_tensor", lo, srcap, hi, ALU.subtract,
                  inc=(k == 15))
        return c["dve"]

    conv_units = []
    for wn in ("wff", "wk", "wv", "wq", "wo"):
        for ot in range(16):
            for piece in range(2):
                conv_units.append((wn, ot, piece))
    conv_done_wc = {}
    conv_state = dict(i=0, par=0, last_conv_dve=[0, 0], cb_free_wc=0)

    def pump(stage_base=None):
        if conv_state["i"] >= len(conv_units):
            return
        wn, ot, piece = conv_units[conv_state["i"]]
        conv_state["i"] += 1
        par = conv_state["par"]
        conv_state["par"] ^= 1
        src = w_in[wn][ot][:, piece * 1024:piece * 1024 + 1024]
        if stage_base is None:
            stage = kv_sb[:, par * 1024:(par + 1) * 1024]
        else:
            stage = stage_base[:, par * 1024:(par + 1) * 1024]
        wait(SC, "dve", conv_state["last_conv_dve"][par])
        gp_dma(stage, src, SC)
        wait(VE, "wc", conv_state["cb_free_wc"])
        wait(VE, "wc", c["wc"])
        ve_op("tensor_copy", cb[:, 0:1024], stage)
        ve_op("tensor_tensor", cb[:, 1024:2048], stage, cb[:, 0:1024],
              ALU.subtract, inc=True)
        conv_state["last_conv_dve"][par] = c["dve"]
        wait(SC, "dve", c["dve"])
        gp_dma(whlo[wn][ot][:, piece * 1024:piece * 1024 + 1024], cb[:, 0:1024], SC)
        gp_dma(whlo[wn][ot][:, 2048 + piece * 1024:2048 + piece * 1024 + 1024],
               cb[:, 1024:2048], SC)
        conv_state["cb_free_wc"] = c["wc"]
        if piece == 1:
            conv_done_wc[(wn, ot)] = c["wc"]

    def slab_ready(wn, ot):
        return conv_done_wc.get((wn, ot), 0)

    sy_dma(xs[:, :], x_in[:, :], "wx")
    sy_dma(eye, eye_d[:, :], "wx")
    for j, d_ in enumerate((bff, scf, bq, bk, bv, bo)):
        sy_dma(d_, bias_d[j], "wx")
    ve_op("memset", ones_col, 1.0)
    ve_op("memset", ones_row, 1.0)
    ve_op("memset", negones_row, -1.0)
    ve_op("memset", onesb[:, :], 1.0)
    wait(VE, "wx", c["wx"])
    ve_op("tensor_scalar", b_h, bff, 0.5, None, ALU.mult)
    ve_op("tensor_scalar", sc_h, scf, 0.5, None, ALU.mult)
    ve_op("tensor_scalar", nbq, bq, -1.0, None, ALU.mult)
    ve_op("tensor_scalar", nbk, bk, -1.0, None, ALU.mult, inc=True)

    def stats_phase(do_pump=False):
        wait(PE, "dve", c["dve"])
        wait(SC, "dve", c["dve"])
        wait(VE, "act", c["act"])
        sq_parity_last_mm = [0, 0]
        prev_math_dve = [0]
        for tc in range(NT):
            tsl = slice(tc * TC, tc * TC + TC)
            pe_use_bank(5)
            pe_use_bank(6)
            for k in range(16):
                xsl = xs[:, k * T + tc * TC: k * T + tc * TC + TC]
                sq_dst = ep[:, (k % 2) * TC:(k % 2) * TC + TC]
                wait(SC, "mm", sq_parity_last_mm[k % 2])
                if k == 0:
                    wait(SC, "dve", prev_math_dve[0])
                if tc == 0 and k < 2:
                    wait(SC, "wx", c["wx"])
                sc_op("activation", sq_dst, xsl, AF.Square, inc=True)
                if tc == 0 and k == 0:
                    wait(PE, "wx", c["wx"])
                mm(pb[5][0:1, 0:TC], ones_col, xsl, start=(k == 0), stop=(k == 15))
                wait(PE, "act", c["act"])
                mm(pb[6][0:1, 0:TC], ones_col, sq_dst, start=(k == 0), stop=(k == 15),
                   inc=True)
                sq_parity_last_mm[k % 2] = c["mm"]
                if do_pump and k % 2 == 0:
                    pump()
            wait(VE, "mm", c["mm"])
            ve_op("tensor_scalar", r0, pb[5][0:1, 0:TC], INV_D, None, ALU.mult)
            ve_op("tensor_scalar", r1, pb[6][0:1, 0:TC], INV_D, None, ALU.mult,
                  inc=True)
            pe_sum_free = ("dve", c["dve"])
            ve_op("tensor_tensor", r2, r0, r0, ALU.mult)
            ve_op("tensor_tensor", r3, r1, r2, ALU.subtract)
            ve_op("tensor_scalar", r2, r3, EPS_LN, None, ALU.add, inc=True)
            wait(SC, "dve", c["dve"])
            sc_op("activation", r3, r2, AF.Sqrt, inc=True)
            wait(VE, "act", c["act"])
            ve_op("reciprocal", r2, r3)
            ve_op("tensor_tensor", r3, r0, r2, ALU.mult, inc=True)
            prev_math_dve[0] = c["dve"]
            wait(PE, "dve", c["dve"])
            wait(PE, *pe_sum_free)
            mm(pb[5][:, 0:TC], negones_row, r3, start=True, stop=True, inc=True)
            mm(pb[6][:, 0:TC], ones_row, r2, start=True, stop=True, inc=True)
            wait(VE, "mm", c["mm"])
            ve_op("tensor_copy", nmr_b[:, tsl], pb[5][:, 0:TC])
            ve_op("tensor_copy", rs_b[:, tsl], pb[6][:, 0:TC], inc=True)
            bank_free[5] = ("dve", c["dve"])
            bank_free[6] = ("dve", c["dve"])

    def ffn_phase(compute_wsum, do_pump=False):
        wait(VE, "act", c["act"])
        last_flush = [0]
        act3_last = [0, 0]
        PA = (ep_a, ep_e0)
        PY = (ep_y, ep_e1)
        xconv_last_mm = [0]
        for tc in range(NT):
            csl = slice(tc * TC, tc * TC + TC)
            xconv_dve = convert_blocks(
                lambda k, tc=tc: xs[:, k * T + tc * TC: k * T + tc * TC + TC],
                xconv_last_mm[0])
            flush_wait_act = 0
            for ot in range(16):
                p, w = slab_load(whlo["wff"][ot], slab_ready("wff", ot))
                wait(PE, *w)
                if ot == 0:
                    wait(PE, "dve", xconv_dve)
                do_ws = compute_wsum and tc == 0
                if do_ws:
                    wsum_mms(p)
                bank = proj3(p)
                slab_last_use[p] = c["mm"]
                if do_ws:
                    wsum_fin(wsum_all[:, ot:ot + 1])
                a_r, y_r = PA[ot % 2], PY[ot % 2]
                wait(VE, "mm", c["mm"])
                if act3_last[ot % 2]:
                    wait(VE, "act", act3_last[ot % 2])
                ve_op("tensor_tensor", a_r, pb[bank][:, 0:TC], rs_b[:, csl], ALU.mult,
                      inc=True)
                bank_free[bank] = ("dve", c["dve"])
                ve_op("scalar_tensor_tensor", y_r, nmr_b[:, csl], wsum_all[:, ot:ot + 1],
                      a_r, ALU.mult, ALU.add, inc=True)
                wait(SC, "dve", c["dve"])
                if last_flush[0]:
                    wait(SC, "dve", last_flush[0])
                sc_op("activation", big[:, ot * TC:(ot + 1) * TC], y_r, AF.Identity,
                      bias=b_h[:, ot:ot + 1], scale=sc_h[:, ot:ot + 1], inc=True)
                act3_last[ot % 2] = c["act"]
                flush_wait_act = c["act"]
                if do_pump and ot % 2 == 0:
                    pump()
            xconv_last_mm[0] = c["mm"]
            wait(VE, "act", flush_wait_act)
            for ot in range(16):
                xsl = xs[:, ot * T + tc * TC: ot * T + tc * TC + TC]
                ve_op("tensor_tensor", xsl, xsl, big[:, ot * TC:(ot + 1) * TC],
                      ALU.add, inc=(ot == 15))
            last_flush[0] = c["dve"]

    def fold_chunk(bank, tsl, wsum_ap, dest, bias_ap, elu, nbias_ap=None):
        wait(VE, "mm", c["mm"])
        ve_op("tensor_tensor", ep_a, pb[bank][:, 0:TC], rs_b[:, tsl], ALU.mult,
              inc=True)
        bank_free[bank] = ("dve", c["dve"])
        ve_op("scalar_tensor_tensor", ep_y, nmr_b[:, tsl], wsum_ap, ep_a,
              ALU.mult, ALU.add, inc=True)
        if elu:
            wait(SC, "dve", c["dve"])
            sc_op("activation", ep_e0, ep_y, AF.Relu, bias=nbias_ap, scale=-1.0)
            sc_op("activation", ep_e1, ep_e0, AF.Exp, scale=-1.0)
            sc_op("activation", ep_e0, ep_y, AF.Relu, bias=bias_ap, scale=1.0,
                  inc=True)
            wait(VE, "act", c["act"])
            ve_op("tensor_tensor", dest, ep_e1, ep_e0, ALU.add, inc=True)
        else:
            ve_op("tensor_scalar", dest, ep_y, bias_ap, None, ALU.add, inc=True)

    def attn_phase():
        wait(VE, "act", c["act"])
        kfm_k = big[:, 0:TC]
        kfm_v = big[:, TC:2 * TC]
        kp_off = 2 * TC
        vt_off = 2 * TC + 256
        for tau in range(2):
            co = vt_off + tau * 129 + 128
            ve_op("memset", big[:, co:co + 1], 1.0, inc=(tau == 1))
        ones_ready = c["dve"]

        xconv_last_mm = [0]
        for tc in range(NT):
            tsl = slice(tc * TC, tc * TC + TC)
            xconv_dve = convert_blocks(
                lambda k, tc=tc: xs[:, k * T + tc * TC: k * T + tc * TC + TC],
                xconv_last_mm[0])
            for h in range(16):
                do_ws = (tc == 0)
                if tc == 0:
                    pump(big[:, 1536:3584])
                    pump(big[:, 1536:3584])
                fold_done = {}
                for proj, wn, bias_ap, wsums, dest in (
                    ("k", "wk", bk[:, h:h + 1], kwsums, kfm_k),
                    ("v", "wv", bv[:, h:h + 1], vwsums, kfm_v),
                ):
                    p, w = slab_load(whlo[wn][h], slab_ready(wn, h))
                    wait(PE, *w)
                    if tc == 0 and h == 0 and proj == "k":
                        wait(PE, "dve", ones_ready)
                    if h == 0 and proj == "k":
                        wait(PE, "dve", xconv_dve)
                    if do_ws:
                        wsum_mms(p)
                    bank = proj3(p)
                    slab_last_use[p] = c["mm"]
                    if do_ws:
                        wsum_fin(wsums[:, h:h + 1])
                    fold_chunk(bank, tsl, wsums[:, h:h + 1], dest, bias_ap,
                               elu=(proj == "k"), nbias_ap=nbk[:, h:h + 1])
                    fold_done[proj] = c["dve"]
                for proj, src, tmo, bw in (("k", kfm_k, kp_off, 128),
                                           ("v", kfm_v, vt_off, 129)):
                    wait(PE, "dve", fold_done[proj])
                    for j in range(2):
                        bnk = 5 + j
                        if bank_free[bnk] is not None:
                            wait(PE, *bank_free[bnk])
                        transpose(pb[bnk][:, 0:128], src[:, j * 128:(j + 1) * 128],
                                  eye, inc=(j == 1))
                    wait(SC, "mm", c["mm"])
                    for j in range(2):
                        co = tmo + j * bw
                        sc_op("activation", big[:, co:co + 128],
                              pb[5 + j][:, 0:128], AF.Identity, inc=(j == 1))
                    bank_free[5] = ("act", c["act"])
                    bank_free[6] = ("act", c["act"])
                if bank_free[7] is not None:
                    wait(PE, *bank_free[7])
                wait(PE, "act", c["act"])
                mm(pb[7][:, 0:129], big[:, kp_off:kp_off + 128],
                   big[:, vt_off:vt_off + 129], start=True, stop=False)
                mm(pb[7][:, 0:129], big[:, kp_off + 128:kp_off + 256],
                   big[:, vt_off + 129:vt_off + 258], start=False, stop=True,
                   inc=True)
                wait(VE, "mm", c["mm"])
                if tc == 0:
                    ve_op("tensor_copy", kv_sb[:, h * 129:(h + 1) * 129],
                          pb[7][:, 0:129], inc=True)
                else:
                    ve_op("tensor_tensor", kv_sb[:, h * 129:(h + 1) * 129],
                          kv_sb[:, h * 129:(h + 1) * 129], pb[7][:, 0:129],
                          ALU.add, inc=True)
                bank_free[7] = ("dve", c["dve"])
            xconv_last_mm[0] = c["mm"]

        wait(GP, "dve", c["dve"])
        c["g"] += 16
        GP.append(lambda e: e.dma_start(out=kvin[:, :], in_=kv_sb[:, :]).then_inc(SEM["g"], 16))
        wait(GP, "g", c["g"])
        c["cc"] += 1
        def cc_fn(e):
            e.collective_compute(
                "AllReduce", ALU.add,
                replica_groups=[[0, 1], [2, 3], [4, 5], [6, 7]],
                ins=[kvin.ap()], outs=[kvout.ap()],
            ).then_inc(SEM["cc"], 1)
        GP.append(cc_fn)
        wait(GP, "cc", c["cc"])
        c["g"] += 16
        GP.append(lambda e: e.dma_start(out=kv_sb[:, :], in_=kvout[:, :]).then_inc(SEM["g"], 16))
        kv_ready = c["g"]

        zr0 = (ep[32:33, 0:TC], ep[64:65, 0:TC])
        zr1 = (ep[0:1, 3 * TC:4 * TC], ep[0:1, 0:TC])
        zrb = (ep[:, TC:2 * TC], ep[:, 2 * TC:3 * TC])
        for tc in range(NT):
            tsl = slice(tc * TC, tc * TC + TC)
            xconv_dve = convert_blocks(
                lambda k, tc=tc: xs[:, k * T + tc * TC: k * T + tc * TC + TC],
                xconv_last_mm[0])
            for dt in range(16):
                p, w = slab_load(whlo["wq"][dt], slab_ready("wq", dt))
                wait(PE, *w)
                if dt == 0:
                    wait(PE, "dve", xconv_dve)
                do_ws = (tc == 0)
                if do_ws:
                    wsum_mms(p)
                bank = proj3(p)
                slab_last_use[p] = c["mm"]
                if do_ws:
                    wsum_fin(wsq_all[:, dt:dt + 1])
                fold_chunk(bank, tsl, wsq_all[:, dt:dt + 1],
                           big[:, dt * TC:(dt + 1) * TC], bq[:, dt:dt + 1], elu=True,
                           nbias_ap=nbq[:, dt:dt + 1])
            qp_done = c["dve"]
            if tc == 0:
                wait(PE, "g", kv_ready)
            wait(PE, "dve", qp_done)
            nb = (4, 0)
            zb = (5, 7)
            bb = (6, 1)
            zrec_done = [0] * 16
            zrecb_mm = [0] * 16

            def pe_numer_z(h):
                par = h % 2
                pe_use_bank(nb[par])
                mm(pb[nb[par]][:, 0:TC], kv_sb[:, h * 129:h * 129 + 128],
                   big[:, h * TC:(h + 1) * TC], start=True, stop=True)
                pe_use_bank(zb[par])
                mm(pb[zb[par]][0:1, 0:TC], kv_sb[:, h * 129 + 128:h * 129 + 129],
                   big[:, h * TC:(h + 1) * TC], start=True, stop=True, inc=True)

            def ve_zrec(h):
                par = h % 2
                wait(VE, "mm", c["mm"])
                ve_op("tensor_scalar", zr0[par], pb[zb[par]][0:1, 0:TC], EPS_AT,
                      None, ALU.add)
                ve_op("reciprocal", zr1[par], zr0[par], inc=True)
                zrec_done[h] = c["dve"]
                bank_free[zb[par]] = ("dve", c["dve"])

            def pe_zrecb(h):
                par = h % 2
                wait(PE, "dve", zrec_done[h])
                pe_use_bank(bb[par])
                mm(pb[bb[par]][:, 0:TC], ones_row, zr1[par], start=True, stop=True,
                   inc=True)
                zrecb_mm[h] = c["mm"]

            def ve_attn(h):
                par = h % 2
                wait(VE, "mm", zrecb_mm[h])
                ve_op("tensor_copy", zrb[par], pb[bb[par]][:, 0:TC])
                bank_free[bb[par]] = ("dve", c["dve"] + 1)
                ve_op("tensor_tensor", big[:, h * TC:(h + 1) * TC],
                      pb[nb[par]][:, 0:TC], zrb[par], ALU.mult, inc=True)
                bank_free[nb[par]] = ("dve", c["dve"])

            def ve_aconv(h):
                src = big[:, h * TC:(h + 1) * TC]
                ve_op("tensor_copy", xhi[:, h * TC:(h + 1) * TC], src)
                ve_op("tensor_tensor", xlo[:, h * TC:(h + 1) * TC], src,
                      xhi[:, h * TC:(h + 1) * TC], ALU.subtract, inc=True)

            pe_numer_z(0)
            ve_zrec(0)
            for h in range(1, 16):
                pe_numer_z(h)
                pe_zrecb(h - 1)
                ve_zrec(h)
                ve_attn(h - 1)
                ve_aconv(h - 1)
            pe_zrecb(15)
            ve_attn(15)
            ve_aconv(15)
            aconv_dve = c["dve"]
            for ct in range(16):
                p, w = slab_load(whlo["wo"][ct], slab_ready("wo", ct))
                wait(PE, *w)
                if ct == 0:
                    wait(PE, "dve", aconv_dve)
                bank = proj3(p)
                slab_last_use[p] = c["mm"]
                xsl = xs[:, ct * T + tc * TC: ct * T + tc * TC + TC]
                wait(VE, "mm", c["mm"])
                ve_op("scalar_tensor_tensor", xsl, pb[bank][:, 0:TC], bo[:, ct:ct + 1],
                      xsl, ALU.add, ALU.add, inc=True)
                bank_free[bank] = ("dve", c["dve"])
            xconv_last_mm[0] = c["mm"]

    if "statsdump" in DO_PHASES:
        stats_phase()
        wait(SY, "dve", c["dve"])
        c["st"] += 32
        SY.append(lambda e: e.dma_start(out=x_out[:, 0:2048], in_=nmr_b[:, :]).then_inc(SEM["st"], 16))
        SY.append(lambda e: e.dma_start(out=x_out[:, 2048:4096], in_=rs_b[:, :]).then_inc(SEM["st"], 16))
    if "ffn1" in DO_PHASES:
        stats_phase(do_pump=True)
        ffn_phase(compute_wsum=True, do_pump=True)
    if "attn" in DO_PHASES:
        stats_phase()
        attn_phase()
    if "ffn2" in DO_PHASES:
        stats_phase()
        ffn_phase(compute_wsum=("ffn1" not in DO_PHASES))

    while conv_state["i"] < len(conv_units):
        pump()

    if "statsdump" not in DO_PHASES:
        wait(SY, "dve", c["dve"])
        c["st"] += 16
        SY.append(lambda e: e.dma_start(out=x_out[:, :], in_=xs[:, :]).then_inc(SEM["st"], 16))
    wait(SY, "st", c["st"])

    with stk, nc.Block() as block:
        @block.sync
        def _(sync):
            for f in SY:
                f(sync)

        @block.tensor
        def _(tensor):
            for f in PE:
                f(tensor)

        @block.vector
        def _(vector):
            for f in VE:
                f(vector)

        @block.scalar
        def _(scalar):
            for f in SC:
                f(scalar)

        @block.gpsimd
        def _(gpsimd):
            for f in GP:
                f(gpsimd)

    return nc


def _tile_w(w):
    wT = np.ascontiguousarray(np.asarray(w, dtype=np.float32).T)
    return np.ascontiguousarray(
        wT.reshape(16, 128, 16, 128).transpose(2, 1, 0, 3).reshape(16, 128, 2048))


def _tile_b(b):
    return np.ascontiguousarray(np.asarray(b, dtype=np.float32).reshape(16, 128).T)


def kernel(L0_attn_hidden_states, L0_attn_q_w, L0_attn_q_b, L0_attn_k_w,
           L0_attn_k_b, L0_attn_v_w, L0_attn_v_b, L0_attn_o_w, L0_attn_o_b,
           L0_ffn_weight, L0_ffn_scale_weight, L0_ffn_bias):
    x = np.asarray(L0_attn_hidden_states, dtype=np.float32)
    bias6 = np.stack([
        _tile_b(L0_ffn_bias), _tile_b(L0_ffn_scale_weight), _tile_b(L0_attn_q_b),
        _tile_b(L0_attn_k_b), _tile_b(L0_attn_v_b), _tile_b(L0_attn_o_b)])
    common = {
        "wff": _tile_w(L0_ffn_weight),
        "wq": _tile_w(L0_attn_q_w),
        "wk": _tile_w(L0_attn_k_w),
        "wv": _tile_w(L0_attn_v_w),
        "wo": _tile_w(L0_attn_o_w),
        "bias6": np.ascontiguousarray(bias6),
        "eye": np.eye(128, dtype=np.float32),
    }
    in_maps = []
    for i in range(8):
        b, half = i // 2, i % 2
        xc = x[b, half * T:(half + 1) * T, :]
        xf = np.ascontiguousarray(
            xc.T.reshape(16, 128, T).transpose(1, 0, 2).reshape(128, KC * T))
        in_maps.append(dict(common, x_in=xf))
    nc = build_nc()
    res = run_bass_kernel_spmd(nc, in_maps, core_ids=list(range(8)))
    out = np.empty((4, 4096, D), dtype=np.float32)
    for i in range(8):
        b, half = i // 2, i % 2
        o = res.results[i]["x_out"]
        xT = o.reshape(128, 16, T).transpose(1, 0, 2).reshape(D, T)
        out[b, half * T:(half + 1) * T, :] = xT.T
    return out


# version 08 (final)
# speedup vs baseline: 1.5263x; 1.0836x over previous
import numpy as np
import concourse.bass as bass
import concourse.mybir as mybir
from concourse.bass_utils import run_bass_kernel_spmd

F32 = mybir.dt.float32
BF16 = mybir.dt.bfloat16
AF = mybir.ActivationFunctionType
ALU = mybir.AluOpType

D = 2048
KC = 16
T = 2048
H = 16
TC = 256
NT = T // TC
EPS_LN = 1e-5
EPS_AT = 1e-6
INV_D = 1.0 / D

DO_PHASES = ("ffn1", "attn", "ffn2")


def build_nc():
    nc = bass.Bass("TRN2", target_bir_lowering=False, detect_race_conditions=False,
                   dynamic_dma_scratch_size=4096)

    x_in = nc.dram_tensor("x_in", [128, KC * T], F32, kind="ExternalInput").ap()
    x_out = nc.dram_tensor("x_out", [128, KC * T], F32, kind="ExternalOutput").ap()
    w_in = {}
    whlo = {}
    for wn in ("wff", "wq", "wk", "wv", "wo"):
        w_in[wn] = nc.dram_tensor(wn, [16, 128, 2048], F32, kind="ExternalInput").ap()
        whlo[wn] = nc.dram_tensor(wn + "_hl", [16, 128, 4096], BF16).ap()
    bias_d = nc.dram_tensor("bias6", [6, 128, 16], F32, kind="ExternalInput").ap()
    eye_d = nc.dram_tensor("eye", [128, 128], F32, kind="ExternalInput").ap()
    kvin = nc.dram_tensor("kvin", [128, H * 129], F32)
    kvout = nc.dram_tensor("kvout", [128, H * 129], F32)

    import contextlib
    stk = contextlib.ExitStack()

    def sb(name, shape, dt=F32):
        return stk.enter_context(nc.sbuf_tensor(name, shape, dt))

    xs = sb("xs", [128, KC * T])
    slab = sb("slab", [128, 2 * 4096], BF16)
    xhi = sb("xhi", [128, KC * TC], BF16)
    xlo = sb("xlo", [128, KC * TC], BF16)
    big = sb("big", [128, KC * TC])
    kv_sb = sb("kv_sb", [128, H * 129])
    nmr_b = sb("nmr_b", [128, T])
    rs_b = sb("rs_b", [128, T])
    ep = sb("ep", [128, 4 * TC])
    cb = sb("cb", [128, 2048], BF16)
    onesb = sb("onesb", [128, 8], BF16)
    arena = sb("arena", [128, 768])

    eye = arena[:, 0:128]
    ones_col = arena[:, 128:129]
    ones_row = arena[0:1, 144:272]
    negones_row = arena[0:1, 272:400]
    bff = arena[:, 400:416]
    scf = arena[:, 416:432]
    bq = arena[:, 432:448]
    bk = arena[:, 448:464]
    bv = arena[:, 464:480]
    bo = arena[:, 480:496]
    b_h = arena[:, 496:512]
    sc_h = arena[:, 512:528]
    wsum_all = arena[:, 528:544]
    wsq_all = arena[:, 544:560]
    kwsums = arena[:, 560:576]
    vwsums = arena[:, 576:592]
    wsv = arena[0:1, 592:720]
    nbq = arena[:, 720:736]
    nbk = arena[:, 736:752]

    ep_a = ep[:, 0 * TC:1 * TC]
    ep_y = ep[:, 1 * TC:2 * TC]
    ep_e0 = ep[:, 2 * TC:3 * TC]
    ep_e1 = ep[:, 3 * TC:4 * TC]
    r0 = ep[0:1, 0 * TC:1 * TC]
    r1 = ep[0:1, 1 * TC:2 * TC]
    r2 = ep[0:1, 2 * TC:3 * TC]
    r3 = ep[0:1, 3 * TC:4 * TC]

    pb = [stk.enter_context(nc.psum_tensor(f"pb{i}", [128, 512], F32)) for i in range(8)]

    SEM = {}
    for s in ("w0", "w1", "wx", "wc", "st", "mm", "dve", "act", "g", "cc"):
        SEM[s] = stk.enter_context(nc.semaphore(s))

    c = dict(mm=0, dve=0, act=0, w0=0, w1=0, wx=0, wc=0, g=0, cc=0, st=0)
    SY, PE, VE, SC, GP = [], [], [], [], []
    bank_free = {i: None for i in range(8)}
    slab_par = [0]
    slab_last_use = [0, 0]

    def wait(lst, sem, val):
        if val > 0:
            lst.append(lambda e, s=SEM[sem], v=val: e.wait_ge(s, v))

    def mm(out, lhsT, rhs, start, stop, inc=False):
        v = 0
        if inc:
            c["mm"] += 1
            v = c["mm"]
        def f(e, o=out, l=lhsT, r=rhs, s=start, p=stop, iv=v):
            i = e.matmul(o, l, r, start=s, stop=p)
            if iv:
                i.then_inc(SEM["mm"], 1)
        PE.append(f)

    def transpose(out, in_, ident, inc=False):
        v = 0
        if inc:
            c["mm"] += 1
            v = c["mm"]
        def f(e, o=out, i_=in_, idn=ident, iv=v):
            i = e.transpose(o, i_, idn)
            if iv:
                i.then_inc(SEM["mm"], 1)
        PE.append(f)

    def ve_op(kind, *args, inc=False):
        v = 0
        if inc:
            c["dve"] += 1
            v = c["dve"]
        def f(e, k=kind, a=args, iv=v):
            i = getattr(e, k)(*a)
            if iv:
                i.then_inc(SEM["dve"], 1)
        VE.append(f)

    def sc_op(kind, *args, inc=False, **kw):
        v = 0
        if inc:
            c["act"] += 1
            v = c["act"]
        def f(e, k=kind, a=args, kw=kw, iv=v):
            i = getattr(e, k)(*a, **kw)
            if iv:
                i.then_inc(SEM["act"], 1)
        SC.append(f)

    def sy_dma(dst, src, sem):
        c[sem] += 16
        def f(e, d=dst, s=src, sm=SEM[sem]):
            e.dma_start(out=d, in_=s).then_inc(sm, 16)
        SY.append(f)

    def gp_dma(dst, src, lst=None):
        c["wc"] += 16
        def f(e, d=dst, s=src):
            e.dma_start(out=d, in_=s).then_inc(SEM["wc"], 16)
        (GP if lst is None else lst).append(f)

    def slab_load(src, ready_wc=0):
        p = slab_par[0] % 2
        slab_par[0] += 1
        Q = SY
        wait(Q, "mm", slab_last_use[p])
        if ready_wc:
            wait(Q, "wc", ready_wc)
        sem = "w0" if p == 0 else "w1"
        c[sem] += 16
        def f(e, d=slab[:, p * 4096:(p + 1) * 4096], s=src, sm=SEM[sem]):
            e.dma_start(out=d, in_=s).then_inc(sm, 16)
        Q.append(f)
        return p, (sem, c[sem])

    def pe_use_bank(i):
        if bank_free[i] is not None:
            wait(PE, *bank_free[i])

    import os
    BARRIER = int(os.environ.get("DK_BARRIER", "0"))

    def barrier():
        wait(PE, "dve", c["dve"])
        wait(PE, "act", c["act"])
        wait(PE, "wc", c["wc"])
        wait(VE, "mm", c["mm"])
        wait(VE, "act", c["act"])
        wait(VE, "wc", c["wc"])
        wait(SC, "mm", c["mm"])
        wait(SC, "dve", c["dve"])

    g_ctr = [0]

    def proj3(slab_p, bank_override=None):
        if bank_override is None:
            bank = g_ctr[0] % 4
            g_ctr[0] += 1
        else:
            bank = bank_override
        pe_use_bank(bank)
        base = slab_p * 4096
        for k in range(16):
            lh = slab[:, base + k * 128:base + (k + 1) * 128]
            ll = slab[:, base + 2048 + k * 128:base + 2048 + (k + 1) * 128]
            rh = xhi[:, k * TC:(k + 1) * TC]
            rl = xlo[:, k * TC:(k + 1) * TC]
            mm(pb[bank][:, 0:TC], lh, rh, start=(k == 0), stop=False)
            mm(pb[bank][:, 0:TC], ll, rh, start=False, stop=False)
            mm(pb[bank][:, 0:TC], lh, rl, start=False, stop=(k == 15),
               inc=(k == 15))
        return bank

    def wsum_mms(slab_p):
        pe_use_bank(4)
        base = slab_p * 4096
        for k in range(16):
            mm(pb[4][0:1, 0:128], onesb[:, 0:1],
               slab[:, base + k * 128:base + (k + 1) * 128],
               start=(k == 0), stop=False)
            mm(pb[4][0:1, 0:128], onesb[:, 0:1],
               slab[:, base + 2048 + k * 128:base + 2048 + (k + 1) * 128],
               start=False, stop=(k == 15), inc=(k == 15))
        wait(VE, "mm", c["mm"])
        ve_op("tensor_copy", wsv, pb[4][0:1, 0:128], inc=True)

    def wsum_fin(dest_col):
        wait(PE, "dve", c["dve"])
        transpose(pb[4][:, 0:1], wsv, eye[0:1, 0:1], inc=True)
        wait(VE, "mm", c["mm"])
        ve_op("tensor_copy", dest_col, pb[4][:, 0:1], inc=True)
        bank_free[4] = ("dve", c["dve"])

    def convert_blocks(src_fn, guard_mm):
        wait(VE, "mm", guard_mm)
        for k in range(16):
            srcap = src_fn(k)
            hi = xhi[:, k * TC:(k + 1) * TC]
            lo = xlo[:, k * TC:(k + 1) * TC]
            ve_op("tensor_copy", hi, srcap)
            ve_op("tensor_tensor", lo, srcap, hi, ALU.subtract,
                  inc=(k == 15))
        return c["dve"]

    conv_units = []
    for wn in ("wff", "wk", "wv", "wq", "wo"):
        for ot in range(16):
            for piece in range(2):
                conv_units.append((wn, ot, piece))
    conv_done_wc = {}
    conv_state = dict(i=0, par=0, last_conv_dve=[0, 0], cb_free_wc=0)

    def pump(stage_base=None):
        if conv_state["i"] >= len(conv_units):
            return
        wn, ot, piece = conv_units[conv_state["i"]]
        conv_state["i"] += 1
        par = conv_state["par"]
        conv_state["par"] ^= 1
        src = w_in[wn][ot][:, piece * 1024:piece * 1024 + 1024]
        if stage_base is None:
            stage = kv_sb[:, par * 1024:(par + 1) * 1024]
        else:
            stage = stage_base[:, par * 1024:(par + 1) * 1024]
        wait(SC, "dve", conv_state["last_conv_dve"][par])
        gp_dma(stage, src, SC)
        wait(VE, "wc", conv_state["cb_free_wc"])
        wait(VE, "wc", c["wc"])
        ve_op("tensor_copy", cb[:, 0:1024], stage)
        ve_op("tensor_tensor", cb[:, 1024:2048], stage, cb[:, 0:1024],
              ALU.subtract, inc=True)
        conv_state["last_conv_dve"][par] = c["dve"]
        wait(SC, "dve", c["dve"])
        gp_dma(whlo[wn][ot][:, piece * 1024:piece * 1024 + 1024], cb[:, 0:1024], SC)
        gp_dma(whlo[wn][ot][:, 2048 + piece * 1024:2048 + piece * 1024 + 1024],
               cb[:, 1024:2048], SC)
        conv_state["cb_free_wc"] = c["wc"]
        if piece == 1:
            conv_done_wc[(wn, ot)] = c["wc"]

    def slab_ready(wn, ot):
        return conv_done_wc.get((wn, ot), 0)

    sy_dma(xs[:, :], x_in[:, :], "wx")
    sy_dma(eye, eye_d[:, :], "wx")
    for j, d_ in enumerate((bff, scf, bq, bk, bv, bo)):
        sy_dma(d_, bias_d[j], "wx")
    ve_op("memset", ones_col, 1.0)
    ve_op("memset", ones_row, 1.0)
    ve_op("memset", negones_row, -1.0)
    ve_op("memset", onesb[:, :], 1.0)
    wait(VE, "wx", c["wx"])
    ve_op("tensor_scalar", b_h, bff, 0.5, None, ALU.mult)
    ve_op("tensor_scalar", sc_h, scf, 0.5, None, ALU.mult)
    ve_op("tensor_scalar", nbq, bq, -1.0, None, ALU.mult)
    ve_op("tensor_scalar", nbk, bk, -1.0, None, ALU.mult, inc=True)

    def stats_phase(do_pump=False):
        wait(PE, "dve", c["dve"])
        wait(SC, "dve", c["dve"])
        wait(VE, "act", c["act"])
        sq_parity_last_mm = [0, 0]
        prev_math_dve = [0]
        for tc in range(NT):
            tsl = slice(tc * TC, tc * TC + TC)
            pe_use_bank(5)
            pe_use_bank(6)
            for k in range(16):
                xsl = xs[:, k * T + tc * TC: k * T + tc * TC + TC]
                sq_dst = ep[:, (k % 2) * TC:(k % 2) * TC + TC]
                eng, lst, sem = ((SC, sc_op, "act") if k % 2 == 0
                                 else (VE, ve_op, "dve"))
                wait(eng, "mm", sq_parity_last_mm[k % 2])
                if k == 0:
                    wait(SC, "dve", prev_math_dve[0])
                if k == 1:
                    wait(VE, "act", c["act"])
                if tc == 0 and k < 2:
                    wait(eng, "wx", c["wx"])
                if k % 2 == 0:
                    sc_op("activation", sq_dst, xsl, AF.Square, inc=True)
                else:
                    ve_op("tensor_tensor", sq_dst, xsl, xsl, ALU.mult, inc=True)
                if tc == 0 and k == 0:
                    wait(PE, "wx", c["wx"])
                mm(pb[5][0:1, 0:TC], ones_col, xsl, start=(k == 0), stop=(k == 15))
                wait(PE, sem, c[sem])
                mm(pb[6][0:1, 0:TC], ones_col, sq_dst, start=(k == 0), stop=(k == 15),
                   inc=True)
                sq_parity_last_mm[k % 2] = c["mm"]
                if do_pump and k % 2 == 0:
                    pump()
            wait(VE, "mm", c["mm"])
            ve_op("tensor_scalar", r0, pb[5][0:1, 0:TC], INV_D, None, ALU.mult)
            ve_op("tensor_scalar", r1, pb[6][0:1, 0:TC], INV_D, None, ALU.mult,
                  inc=True)
            pe_sum_free = ("dve", c["dve"])
            ve_op("tensor_tensor", r2, r0, r0, ALU.mult)
            ve_op("tensor_tensor", r3, r1, r2, ALU.subtract)
            ve_op("tensor_scalar", r2, r3, EPS_LN, None, ALU.add, inc=True)
            wait(SC, "dve", c["dve"])
            sc_op("activation", r3, r2, AF.Sqrt, inc=True)
            wait(VE, "act", c["act"])
            ve_op("reciprocal", r2, r3)
            ve_op("tensor_tensor", r3, r0, r2, ALU.mult, inc=True)
            prev_math_dve[0] = c["dve"]
            wait(PE, "dve", c["dve"])
            wait(PE, *pe_sum_free)
            mm(pb[5][:, 0:TC], negones_row, r3, start=True, stop=True, inc=True)
            mm(pb[6][:, 0:TC], ones_row, r2, start=True, stop=True, inc=True)
            wait(VE, "mm", c["mm"])
            ve_op("tensor_copy", nmr_b[:, tsl], pb[5][:, 0:TC])
            ve_op("tensor_copy", rs_b[:, tsl], pb[6][:, 0:TC], inc=True)
            bank_free[5] = ("dve", c["dve"])
            bank_free[6] = ("dve", c["dve"])

    def ffn_phase(compute_wsum, do_pump=False):
        wait(VE, "act", c["act"])
        last_flush = [0]
        act3_last = [0, 0]
        PA = (ep_a, ep_e0)
        PY = (ep_y, ep_e1)
        xconv_last_mm = [0]
        for tc in range(NT):
            csl = slice(tc * TC, tc * TC + TC)
            xconv_dve = convert_blocks(
                lambda k, tc=tc: xs[:, k * T + tc * TC: k * T + tc * TC + TC],
                xconv_last_mm[0])
            flush_wait_act = 0
            for ot in range(16):
                p, w = slab_load(whlo["wff"][ot], slab_ready("wff", ot))
                wait(PE, *w)
                if ot == 0:
                    wait(PE, "dve", xconv_dve)
                do_ws = compute_wsum and tc == 0
                if do_ws:
                    wsum_mms(p)
                bank = proj3(p)
                slab_last_use[p] = c["mm"]
                if do_ws:
                    wsum_fin(wsum_all[:, ot:ot + 1])
                a_r, y_r = PA[ot % 2], PY[ot % 2]
                wait(VE, "mm", c["mm"])
                if act3_last[ot % 2]:
                    wait(VE, "act", act3_last[ot % 2])
                ve_op("tensor_tensor", a_r, pb[bank][:, 0:TC], rs_b[:, csl], ALU.mult,
                      inc=True)
                bank_free[bank] = ("dve", c["dve"])
                ve_op("scalar_tensor_tensor", y_r, nmr_b[:, csl], wsum_all[:, ot:ot + 1],
                      a_r, ALU.mult, ALU.add, inc=True)
                wait(SC, "dve", c["dve"])
                if last_flush[0]:
                    wait(SC, "dve", last_flush[0])
                sc_op("activation", big[:, ot * TC:(ot + 1) * TC], y_r, AF.Identity,
                      bias=b_h[:, ot:ot + 1], scale=sc_h[:, ot:ot + 1], inc=True)
                act3_last[ot % 2] = c["act"]
                flush_wait_act = c["act"]
                if do_pump and ot % 2 == 0:
                    pump()
            xconv_last_mm[0] = c["mm"]
            wait(VE, "act", flush_wait_act)
            for ot in range(16):
                xsl = xs[:, ot * T + tc * TC: ot * T + tc * TC + TC]
                ve_op("tensor_tensor", xsl, xsl, big[:, ot * TC:(ot + 1) * TC],
                      ALU.add, inc=(ot == 15))
            last_flush[0] = c["dve"]

    def fold_chunk(bank, tsl, wsum_ap, dest, bias_ap, elu, nbias_ap=None):
        wait(VE, "mm", c["mm"])
        ve_op("tensor_tensor", ep_a, pb[bank][:, 0:TC], rs_b[:, tsl], ALU.mult,
              inc=True)
        bank_free[bank] = ("dve", c["dve"])
        ve_op("scalar_tensor_tensor", ep_y, nmr_b[:, tsl], wsum_ap, ep_a,
              ALU.mult, ALU.add, inc=True)
        if elu:
            wait(SC, "dve", c["dve"])
            sc_op("activation", ep_e0, ep_y, AF.Relu, bias=nbias_ap, scale=-1.0)
            sc_op("activation", ep_e1, ep_e0, AF.Exp, scale=-1.0)
            sc_op("activation", ep_e0, ep_y, AF.Relu, bias=bias_ap, scale=1.0,
                  inc=True)
            wait(VE, "act", c["act"])
            ve_op("tensor_tensor", dest, ep_e1, ep_e0, ALU.add, inc=True)
        else:
            ve_op("tensor_scalar", dest, ep_y, bias_ap, None, ALU.add, inc=True)

    def attn_phase():
        wait(VE, "act", c["act"])
        kfm_k = big[:, 0:TC]
        kfm_v = big[:, TC:2 * TC]
        kp_off = 2 * TC
        vt_off = 2 * TC + 256
        for tau in range(2):
            co = vt_off + tau * 129 + 128
            ve_op("memset", big[:, co:co + 1], 1.0, inc=(tau == 1))
        ones_ready = c["dve"]

        xconv_last_mm = [0]
        for tc in range(NT):
            tsl = slice(tc * TC, tc * TC + TC)
            xconv_dve = convert_blocks(
                lambda k, tc=tc: xs[:, k * T + tc * TC: k * T + tc * TC + TC],
                xconv_last_mm[0])
            for h in range(16):
                do_ws = (tc == 0)
                if tc == 0:
                    pump(big[:, 1536:3584])
                    pump(big[:, 1536:3584])
                fold_done = {}
                for proj, wn, bias_ap, wsums, dest in (
                    ("k", "wk", bk[:, h:h + 1], kwsums, kfm_k),
                    ("v", "wv", bv[:, h:h + 1], vwsums, kfm_v),
                ):
                    p, w = slab_load(whlo[wn][h], slab_ready(wn, h))
                    wait(PE, *w)
                    if tc == 0 and h == 0 and proj == "k":
                        wait(PE, "dve", ones_ready)
                    if h == 0 and proj == "k":
                        wait(PE, "dve", xconv_dve)
                    if do_ws:
                        wsum_mms(p)
                    bank = proj3(p)
                    slab_last_use[p] = c["mm"]
                    if do_ws:
                        wsum_fin(wsums[:, h:h + 1])
                    fold_chunk(bank, tsl, wsums[:, h:h + 1], dest, bias_ap,
                               elu=(proj == "k"), nbias_ap=nbk[:, h:h + 1])
                    fold_done[proj] = c["dve"]
                for proj, src, tmo, bw in (("k", kfm_k, kp_off, 128),
                                           ("v", kfm_v, vt_off, 129)):
                    wait(PE, "dve", fold_done[proj])
                    for j in range(2):
                        bnk = 5 + j
                        if bank_free[bnk] is not None:
                            wait(PE, *bank_free[bnk])
                        transpose(pb[bnk][:, 0:128], src[:, j * 128:(j + 1) * 128],
                                  eye, inc=(j == 1))
                    wait(SC, "mm", c["mm"])
                    for j in range(2):
                        co = tmo + j * bw
                        sc_op("activation", big[:, co:co + 128],
                              pb[5 + j][:, 0:128], AF.Identity, inc=(j == 1))
                    bank_free[5] = ("act", c["act"])
                    bank_free[6] = ("act", c["act"])
                if bank_free[7] is not None:
                    wait(PE, *bank_free[7])
                wait(PE, "act", c["act"])
                mm(pb[7][:, 0:129], big[:, kp_off:kp_off + 128],
                   big[:, vt_off:vt_off + 129], start=True, stop=False)
                mm(pb[7][:, 0:129], big[:, kp_off + 128:kp_off + 256],
                   big[:, vt_off + 129:vt_off + 258], start=False, stop=True,
                   inc=True)
                wait(VE, "mm", c["mm"])
                if tc == 0:
                    ve_op("tensor_copy", kv_sb[:, h * 129:(h + 1) * 129],
                          pb[7][:, 0:129], inc=True)
                else:
                    ve_op("tensor_tensor", kv_sb[:, h * 129:(h + 1) * 129],
                          kv_sb[:, h * 129:(h + 1) * 129], pb[7][:, 0:129],
                          ALU.add, inc=True)
                bank_free[7] = ("dve", c["dve"])
            xconv_last_mm[0] = c["mm"]

        wait(GP, "dve", c["dve"])
        c["g"] += 16
        GP.append(lambda e: e.dma_start(out=kvin[:, :], in_=kv_sb[:, :]).then_inc(SEM["g"], 16))
        wait(GP, "g", c["g"])
        c["cc"] += 1
        def cc_fn(e):
            e.collective_compute(
                "AllReduce", ALU.add,
                replica_groups=[[0, 1], [2, 3], [4, 5], [6, 7]],
                ins=[kvin.ap()], outs=[kvout.ap()],
            ).then_inc(SEM["cc"], 1)
        GP.append(cc_fn)
        wait(GP, "cc", c["cc"])
        c["g"] += 16
        GP.append(lambda e: e.dma_start(out=kv_sb[:, :], in_=kvout[:, :]).then_inc(SEM["g"], 16))
        kv_ready = c["g"]

        zr0 = (ep[32:33, 0:TC], ep[64:65, 0:TC])
        zr1 = (ep[0:1, 3 * TC:4 * TC], ep[0:1, 0:TC])
        zrb = (ep[:, TC:2 * TC], ep[:, 2 * TC:3 * TC])
        for tc in range(NT):
            tsl = slice(tc * TC, tc * TC + TC)
            xconv_dve = convert_blocks(
                lambda k, tc=tc: xs[:, k * T + tc * TC: k * T + tc * TC + TC],
                xconv_last_mm[0])
            for dt in range(16):
                p, w = slab_load(whlo["wq"][dt], slab_ready("wq", dt))
                wait(PE, *w)
                if dt == 0:
                    wait(PE, "dve", xconv_dve)
                do_ws = (tc == 0)
                if do_ws:
                    wsum_mms(p)
                bank = proj3(p)
                slab_last_use[p] = c["mm"]
                if do_ws:
                    wsum_fin(wsq_all[:, dt:dt + 1])
                fold_chunk(bank, tsl, wsq_all[:, dt:dt + 1],
                           big[:, dt * TC:(dt + 1) * TC], bq[:, dt:dt + 1], elu=True,
                           nbias_ap=nbq[:, dt:dt + 1])
            qp_done = c["dve"]
            if tc == 0:
                wait(PE, "g", kv_ready)
            wait(PE, "dve", qp_done)
            nb = (4, 0)
            zb = (5, 7)
            bb = (6, 1)
            zrec_done = [0] * 16
            zrecb_mm = [0] * 16

            def pe_numer_z(h):
                par = h % 2
                pe_use_bank(nb[par])
                mm(pb[nb[par]][:, 0:TC], kv_sb[:, h * 129:h * 129 + 128],
                   big[:, h * TC:(h + 1) * TC], start=True, stop=True)
                pe_use_bank(zb[par])
                mm(pb[zb[par]][0:1, 0:TC], kv_sb[:, h * 129 + 128:h * 129 + 129],
                   big[:, h * TC:(h + 1) * TC], start=True, stop=True, inc=True)

            def ve_zrec(h):
                par = h % 2
                wait(VE, "mm", c["mm"])
                ve_op("tensor_scalar", zr0[par], pb[zb[par]][0:1, 0:TC], EPS_AT,
                      None, ALU.add)
                ve_op("reciprocal", zr1[par], zr0[par], inc=True)
                zrec_done[h] = c["dve"]
                bank_free[zb[par]] = ("dve", c["dve"])

            def pe_zrecb(h):
                par = h % 2
                wait(PE, "dve", zrec_done[h])
                pe_use_bank(bb[par])
                mm(pb[bb[par]][:, 0:TC], ones_row, zr1[par], start=True, stop=True,
                   inc=True)
                zrecb_mm[h] = c["mm"]

            def ve_attn(h):
                par = h % 2
                wait(SC, "mm", zrecb_mm[h])
                sc_op("copy", zrb[par], pb[bb[par]][:, 0:TC], inc=True)
                bank_free[bb[par]] = ("act", c["act"])
                wait(VE, "act", c["act"])
                ve_op("tensor_tensor", big[:, h * TC:(h + 1) * TC],
                      pb[nb[par]][:, 0:TC], zrb[par], ALU.mult, inc=True)
                bank_free[nb[par]] = ("dve", c["dve"])

            def ve_aconv(h):
                src = big[:, h * TC:(h + 1) * TC]
                wait(SC, "dve", c["dve"])
                sc_op("copy", xhi[:, h * TC:(h + 1) * TC], src, inc=True)
                wait(VE, "act", c["act"])
                ve_op("tensor_tensor", xlo[:, h * TC:(h + 1) * TC], src,
                      xhi[:, h * TC:(h + 1) * TC], ALU.subtract, inc=True)

            pe_numer_z(0)
            ve_zrec(0)
            for h in range(1, 16):
                pe_numer_z(h)
                pe_zrecb(h - 1)
                ve_zrec(h)
                ve_attn(h - 1)
                ve_aconv(h - 1)
            pe_zrecb(15)
            ve_attn(15)
            ve_aconv(15)
            aconv_dve = c["dve"]
            for ct in range(16):
                p, w = slab_load(whlo["wo"][ct], slab_ready("wo", ct))
                wait(PE, *w)
                if ct == 0:
                    wait(PE, "dve", aconv_dve)
                bank = proj3(p)
                slab_last_use[p] = c["mm"]
                xsl = xs[:, ct * T + tc * TC: ct * T + tc * TC + TC]
                wait(VE, "mm", c["mm"])
                ve_op("scalar_tensor_tensor", xsl, pb[bank][:, 0:TC], bo[:, ct:ct + 1],
                      xsl, ALU.add, ALU.add, inc=True)
                bank_free[bank] = ("dve", c["dve"])
            xconv_last_mm[0] = c["mm"]

    if "statsdump" in DO_PHASES:
        stats_phase()
        wait(SY, "dve", c["dve"])
        c["st"] += 32
        SY.append(lambda e: e.dma_start(out=x_out[:, 0:2048], in_=nmr_b[:, :]).then_inc(SEM["st"], 16))
        SY.append(lambda e: e.dma_start(out=x_out[:, 2048:4096], in_=rs_b[:, :]).then_inc(SEM["st"], 16))
    if "ffn1" in DO_PHASES:
        stats_phase(do_pump=True)
        ffn_phase(compute_wsum=True, do_pump=True)
    if "attn" in DO_PHASES:
        stats_phase()
        attn_phase()
    if "ffn2" in DO_PHASES:
        stats_phase()
        ffn_phase(compute_wsum=("ffn1" not in DO_PHASES))

    while conv_state["i"] < len(conv_units):
        pump()

    if "statsdump" not in DO_PHASES:
        wait(SY, "dve", c["dve"])
        c["st"] += 16
        SY.append(lambda e: e.dma_start(out=x_out[:, :], in_=xs[:, :]).then_inc(SEM["st"], 16))
    wait(SY, "st", c["st"])

    with stk, nc.Block() as block:
        @block.sync
        def _(sync):
            for f in SY:
                f(sync)

        @block.tensor
        def _(tensor):
            for f in PE:
                f(tensor)

        @block.vector
        def _(vector):
            for f in VE:
                f(vector)

        @block.scalar
        def _(scalar):
            for f in SC:
                f(scalar)

        @block.gpsimd
        def _(gpsimd):
            for f in GP:
                f(gpsimd)

    return nc


def _tile_w(w):
    wT = np.ascontiguousarray(np.asarray(w, dtype=np.float32).T)
    return np.ascontiguousarray(
        wT.reshape(16, 128, 16, 128).transpose(2, 1, 0, 3).reshape(16, 128, 2048))


def _tile_b(b):
    return np.ascontiguousarray(np.asarray(b, dtype=np.float32).reshape(16, 128).T)


def kernel(L0_attn_hidden_states, L0_attn_q_w, L0_attn_q_b, L0_attn_k_w,
           L0_attn_k_b, L0_attn_v_w, L0_attn_v_b, L0_attn_o_w, L0_attn_o_b,
           L0_ffn_weight, L0_ffn_scale_weight, L0_ffn_bias):
    x = np.asarray(L0_attn_hidden_states, dtype=np.float32)
    bias6 = np.stack([
        _tile_b(L0_ffn_bias), _tile_b(L0_ffn_scale_weight), _tile_b(L0_attn_q_b),
        _tile_b(L0_attn_k_b), _tile_b(L0_attn_v_b), _tile_b(L0_attn_o_b)])
    common = {
        "wff": _tile_w(L0_ffn_weight),
        "wq": _tile_w(L0_attn_q_w),
        "wk": _tile_w(L0_attn_k_w),
        "wv": _tile_w(L0_attn_v_w),
        "wo": _tile_w(L0_attn_o_w),
        "bias6": np.ascontiguousarray(bias6),
        "eye": np.eye(128, dtype=np.float32),
    }
    in_maps = []
    for i in range(8):
        b, half = i // 2, i % 2
        xc = x[b, half * T:(half + 1) * T, :]
        xf = np.ascontiguousarray(
            xc.T.reshape(16, 128, T).transpose(1, 0, 2).reshape(128, KC * T))
        in_maps.append(dict(common, x_in=xf))
    nc = build_nc()
    res = run_bass_kernel_spmd(nc, in_maps, core_ids=list(range(8)))
    out = np.empty((4, 4096, D), dtype=np.float32)
    for i in range(8):
        b, half = i // 2, i % 2
        o = res.results[i]["x_out"]
        xT = o.reshape(128, 16, T).transpose(1, 0, 2).reshape(D, T)
        out[b, half * T:(half + 1) * T, :] = xT.T
    return out
